# version 01
# baseline (speedup 1.0000x reference)
import numpy as np
import ml_dtypes
from contextlib import ExitStack

import concourse.bass as bass
import concourse.mybir as mybir
from concourse.bass_utils import run_bass_kernel_spmd

F32 = mybir.dt.float32
BF16 = mybir.dt.bfloat16
F16 = mybir.dt.float16
AF = mybir.ActivationFunctionType
ALU = mybir.AluOpType
AX = mybir.AxisListType
BF = ml_dtypes.bfloat16

B, L, D, H, DH = 4, 4096, 1024, 16, 64
HALF = L // 2
DHID = 4096
NCORE = 8



def prep_lhsT(w):
    din, dout = w.shape
    return np.ascontiguousarray(
        w.reshape(din // 128, 128, dout // 128, 128).transpose(2, 1, 0, 3))


def prep_rhs(w):
    din, dout = w.shape
    return np.ascontiguousarray(w.reshape(din // 128, 128, dout))


def prep_pvec(v):
    return np.ascontiguousarray(v.reshape(-1, 128).T)




class Sem:
    def __init__(self, stack, nc, name):
        self.h = stack.enter_context(nc.semaphore(name))
        self.v = 0

    def inc(self, ins, n=1):
        self.v += n
        return ins.then_inc(self.h, n)

    def wait(self, eng, v=None):
        eng.wait_ge(self.h, self.v if v is None else v)


def mm_layer(block, env, *, wdram, n_dc, n_k, rhs_fn, out_fn, func, scale_fn,
             n_free, bias_fn=None, pre_wait=None):
    nc, st, sa = env["nc"], env["st"], env["sa"]
    gws = env["gws"]
    wslab, psMM = env["wslab"], env["psMM"]
    st0, sa0 = st.v, sa.v
    wtarget = [gws[dc % 2].v + 16 * (dc // 2 + 1) for dc in range(n_dc)]

    @block.gpsimd
    def _(g):
        st.wait(g, st0)
        for dc in range(n_dc):
            if dc >= 2:
                st.wait(g, st0 + dc - 1)
            gws[dc % 2].inc(g.dma_start(out=wslab[dc % 2][:, :n_k * 128],
                                        in_=wdram[dc].rearrange("p k c -> p (k c)")), 16)

    @block.tensor
    def _(tensor):
        if pre_wait is not None:
            tensor.wait_ge(pre_wait[0].h, pre_wait[1])
        sa.wait(tensor, sa0)
        for dc in range(n_dc):
            tensor.wait_ge(gws[dc % 2].h, wtarget[dc])
            if dc >= 2:
                sa.wait(tensor, sa0 + dc - 1)
            for k in range(n_k):
                ins = tensor.matmul(psMM[dc % 2][:, :n_free],
                                    wslab[dc % 2][:, k * 128:(k + 1) * 128],
                                    rhs_fn(k), start=(k == 0), stop=(k == n_k - 1))
            st.inc(ins)

    @block.scalar
    def _(scalar):
        for dc in range(n_dc):
            st.wait(scalar, st0 + dc + 1)
            kw = {}
            if scale_fn is not None:
                kw["scale"] = scale_fn(dc)
            if bias_fn is not None:
                kw["bias"] = bias_fn(dc)
            sa.inc(scalar.activation(out=out_fn(dc), in_=psMM[dc % 2][:, :n_free],
                                     func=func, **kw))



def emit_ln(block, env, *, n, x_fn, out_fn, vec_pre=None):
    sv, sa, st8, sq = env["sv"], env["sa"], env["st8"], env["sq"]
    sv0, sa0 = sv.v, sa.v

    @block.vector
    def _(vector):
        for i in range(n):
            if vec_pre is not None:
                vec_pre(vector, i)
            sv.inc(vector.tensor_reduce(out=st8[:, 0:1], in_=x_fn(i), axis=AX.X,
                                        op=ALU.add))
            sv.wait(vector)
            sv.inc(vector.tensor_scalar(out=st8[:, 1:2], in0=st8[:, 0:1],
                                        scalar1=-1.0 / D, scalar2=None, op0=ALU.mult))
            sv.wait(vector)
            sa.wait(vector, sa0 + 3 * i + 1)
            sv.inc(vector.tensor_scalar(out=st8[:, 3:4], in0=st8[:, 2:3],
                                        scalar1=1.0 / D, scalar2=1e-5,
                                        op0=ALU.mult, op1=ALU.add))
            sa.wait(vector, sa0 + 3 * i + 3)
            sv.inc(vector.tensor_scalar(out=out_fn(i), in0=x_fn(i),
                                        scalar1=st8[:, 1:2], scalar2=st8[:, 5:6],
                                        op0=ALU.add, op1=ALU.mult))

    @block.scalar
    def _(scalar):
        for i in range(n):
            sv.wait(scalar, sv0 + 4 * i + 2)
            sa.inc(scalar.activation(out=sq[:], in_=x_fn(i), func=AF.Square,
                                     bias=st8[:, 1:2], accum_out=st8[:, 2:3]))
            sv.wait(scalar, sv0 + 4 * i + 3)
            sa.inc(scalar.activation(out=st8[:, 4:5], in_=st8[:, 3:4], func=AF.Ln))
            sa.wait(scalar)
            sa.inc(scalar.activation(out=st8[:, 5:6], in_=st8[:, 4:5],
                                     func=AF.Exp, scale=-0.5))




def build_B(debug=False):
    GRP = 512
    NG = HALF // GRP
    nc = bass.Bass("TRN2", target_bir_lowering=False)
    x = nc.dram_tensor("x", [HALF, D], F32, kind="ExternalInput").ap()
    y = nc.dram_tensor("y", [HALF, D], BF16, kind="ExternalInput").ap()
    W1 = nc.dram_tensor("W1", [32, 128, 8, 128], F32, kind="ExternalInput").ap()
    W2 = nc.dram_tensor("W2", [32, 128, 8, 128], F32, kind="ExternalInput").ap()
    H1 = nc.dram_tensor("H1", [32, 128, 32, 128], F32, kind="ExternalInput").ap()
    H2 = nc.dram_tensor("H2", [32, 128, 32, 128], F32, kind="ExternalInput").ap()
    W3 = nc.dram_tensor("W3", [8, 128, 32, 128], F32, kind="ExternalInput").ap()
    S1 = nc.dram_tensor("S1", [128, 32], F32, kind="ExternalInput").ap()
    S2 = nc.dram_tensor("S2", [128, 32], F32, kind="ExternalInput").ap()
    SH1 = nc.dram_tensor("SH1", [128, 32], F32, kind="ExternalInput").ap()
    SH2 = nc.dram_tensor("SH2", [128, 32], F32, kind="ExternalInput").ap()
    S3 = nc.dram_tensor("S3", [128, 8], F32, kind="ExternalInput").ap()
    WP = nc.dram_tensor("WP", [8, 128, 1024], F32, kind="ExternalInput").ap()
    BP = nc.dram_tensor("BP", [1, 1024], F32, kind="ExternalInput").ap()
    x1 = nc.dram_tensor("x1", [HALF, D], F32, kind="ExternalOutput").ap()
    if debug:
        dbg_xn = nc.dram_tensor("dbg_xn", [128, 4, 1024], BF16, kind="ExternalOutput").ap()
        dbg_h = nc.dram_tensor("dbg_h", [128, 32, 512], BF16, kind="ExternalOutput").ap()
        dbg_h1 = nc.dram_tensor("dbg_h1", [128, 32, 512], BF16, kind="ExternalOutput").ap()
        dbg_g = nc.dram_tensor("dbg_g", [128, 4, 1024], BF16, kind="ExternalOutput").ap()
        dbg_s = nc.dram_tensor("dbg_s", [128, 4, 1024], F32, kind="ExternalOutput").ap()

    ident_c = nc.inline_tensor(np.eye(128).astype(BF), name="identc").ap()
    ones_c = nc.inline_tensor(np.ones((1, 128), dtype=BF), name="onesc").ap()

    with ExitStack() as stack:
        e = stack.enter_context
        env = {"nc": nc}
        x_g = e(nc.sbuf_tensor("x_g", [128, 4, 1024], F32))
        sq = e(nc.sbuf_tensor("sq", [128, 1024], BF16))
        st8 = e(nc.sbuf_tensor("st8", [128, 8], F32))
        xn_bf = e(nc.sbuf_tensor("xn_bf", [128, 4, 1024], BF16))
        xnT = e(nc.sbuf_tensor("xnT", [128, 8, GRP], BF16))
        hA = e(nc.sbuf_tensor("hA", [128, 32, GRP], BF16))
        hB = e(nc.sbuf_tensor("hB", [128, 32, GRP], BF16))
        gT = e(nc.sbuf_tensor("gT", [128, 8, GRP], BF16))
        g_tok = e(nc.sbuf_tensor("g_tok", [128, 4, 1024], BF16))
        s_tok = e(nc.sbuf_tensor("s_tok", [128, 4, 1024], F32))
        y_sb = e(nc.sbuf_tensor("y_sb", [128, 2, 1024], BF16))
        yT = e(nc.sbuf_tensor("yT", [128, 8, 128], BF16))
        wslab = [e(nc.sbuf_tensor(f"wslab{i}", [128, 4096], BF16)) for i in range(2)]
        WPsb = e(nc.sbuf_tensor("WPsb", [128, 8, 1024], BF16))
        bp_sb = e(nc.sbuf_tensor("bp_sb", [1, 1024], BF16))
        scl = e(nc.sbuf_tensor("scl", [128, 5, 32], F32))
        ident = e(nc.sbuf_tensor("ident", [128, 128], BF16))
        ones = e(nc.sbuf_tensor("ones", [1, 128], BF16))
        x1t = e(nc.sbuf_tensor("x1t", [128, 2, 1024], F32))
        tmp_f = e(nc.sbuf_tensor("tmp_f", [128, 1024], F32))
        psMM = [e(nc.psum_tensor(f"psMM{i}", [128, 512], F32)) for i in range(2)]
        psTR = e(nc.psum_tensor("psTR", [128, 1024], BF16))
        psS = [e(nc.psum_tensor(f"psS{i}", [128, 512], F32)) for i in range(2)]

        sd = Sem(stack, nc, "sd")
        gw = Sem(stack, nc, "gw")
        st = Sem(stack, nc, "st")
        sv = Sem(stack, nc, "sv")
        sa = Sem(stack, nc, "sa")
        gws = [Sem(stack, nc, "gws0"), Sem(stack, nc, "gws1")]
        env.update(sq=sq[:], st8=st8[:], psTR=psTR[:], ident=ident[:],
                   wslab=[w[:] for w in wslab], psMM=[p[:] for p in psMM],
                   sd=sd, gw=gw, st=st, sv=sv, sa=sa, gws=gws)

        with nc.Block() as block:
            @block.sync
            def _(sync):
                sd.inc(sync.dma_start(out=scl[:, 0, :], in_=S1), 16)
                sd.inc(sync.dma_start(out=scl[:, 1, :], in_=S2), 16)
                sd.inc(sync.dma_start(out=scl[:, 2, :], in_=SH1), 16)
                sd.inc(sync.dma_start(out=scl[:, 3, :], in_=SH2), 16)
                sd.inc(sync.dma_start(out=scl[:, 4, 0:8], in_=S3), 16)
                sd.inc(sync.dma_start(out=ident[:], in_=ident_c), 16)
                sd.inc(sync.dma_start(out=ones[:], in_=ones_c), 16)
                sync.wait_ge(sd.h, sd.v)

            @block.gpsimd
            def _(g):
                for k in range(8):
                    gw.inc(g.dma_start(out=WPsb[:, k, :], in_=WP[k]), 16)
                gw.inc(g.dma_start(out=bp_sb[:], in_=BP), 16)
                g.wait_ge(gw.h, gw.v)

            pre_sd, pre_gw = sd.v, gw.v

            @block.tensor
            def _(tensor):
                tensor.wait_ge(sd.h, pre_sd)
                tensor.wait_ge(gw.h, pre_gw)

            @block.scalar
            def _(scalar):
                scalar.wait_ge(sd.h, pre_sd)

            @block.vector
            def _(vector):
                vector.wait_ge(sd.h, pre_sd)

            resid_done = [0]

            for g_i in range(NG):
                t0 = g_i * GRP
                sdA = sd.v
                rd_prev = resid_done[0]

                @block.sync
                def _(sync):
                    if g_i > 0:
                        sv.wait(sync, rd_prev)
                    for i in range(4):
                        sd.inc(sync.dma_start(out=x_g[:, i, :],
                                              in_=x[t0 + i * 128: t0 + (i + 1) * 128, :]), 16)

                svA = sv.v
                emit_ln(block, env, n=4, x_fn=lambda i: x_g[:, i, :],
                        out_fn=lambda i: xn_bf[:, i, :],
                        vec_pre=lambda vector, i: vector.wait_ge(sd.h, sdA + 64))

                stT0, svT0 = st.v, sv.v

                @block.tensor
                def _(tensor):
                    for i in range(4):
                        sv.wait(tensor, svA + 4 * i + 4)
                        sv.wait(tensor, svT0 + i)
                        for dc in range(8):
                            ins = tensor.transpose(psTR[:, dc * 128:(dc + 1) * 128],
                                                   xn_bf[:, i, dc * 128:(dc + 1) * 128],
                                                   ident[:])
                        st.inc(ins)

                @block.vector
                def _(vector):
                    for i in range(4):
                        st.wait(vector, stT0 + i + 1)
                        sv.inc(vector.tensor_copy(
                            out=xnT[:].rearrange("p k t -> p k t")[:, :, i * 128:(i + 1) * 128],
                            in_=psTR[:].rearrange("p (k c) -> p k c", k=8)))

                svXNT = sv.v
                mm_layer(block, env, wdram=W1, n_dc=32, n_k=8,
                         rhs_fn=lambda k: xnT[:, k, :],
                         out_fn=lambda dc: hA[:, dc, :], func=AF.Silu,
                         scale_fn=lambda dc: scl[:, 0, dc:dc + 1], n_free=GRP,
                         pre_wait=(sv, svXNT))
                mm_layer(block, env, wdram=W2, n_dc=32, n_k=8,
                         rhs_fn=lambda k: xnT[:, k, :],
                         out_fn=lambda dc: hB[:, dc, :], func=AF.Copy,
                         scale_fn=lambda dc: scl[:, 1, dc:dc + 1], n_free=GRP,
                         pre_wait=(sv, svXNT))
                saB = sa.v

                @block.vector
                def _(vector):
                    sa.wait(vector, saB)
                    for dc in range(32):
                        ins = vector.tensor_tensor(out=hA[:, dc, :], in0=hA[:, dc, :],
                                                   in1=hB[:, dc, :], op=ALU.mult)
                    sv.inc(ins)

                svMul = sv.v

                if debug and g_i == 0:
                    @block.sync
                    def _(sync):
                        sync.wait_ge(sv.h, svMul)
                        sd.inc(sync.dma_start(out=dbg_h, in_=hA[:]), 16)

                mm_layer(block, env, wdram=H1, n_dc=32, n_k=32,
                         rhs_fn=lambda k: hA[:, k, :],
                         out_fn=lambda dc: hB[:, dc, :], func=AF.Gelu,
                         scale_fn=lambda dc: scl[:, 2, dc:dc + 1], n_free=GRP,
                         pre_wait=(sv, svMul))
                mm_layer(block, env, wdram=H2, n_dc=32, n_k=32,
                         rhs_fn=lambda k: hB[:, k, :],
                         out_fn=lambda dc: hA[:, dc, :], func=AF.Gelu,
                         scale_fn=lambda dc: scl[:, 3, dc:dc + 1], n_free=GRP,
                         pre_wait=(sa, sa.v))
                mm_layer(block, env, wdram=W3, n_dc=8, n_k=32,
                         rhs_fn=lambda k: hA[:, k, :],
                         out_fn=lambda dc: gT[:, dc, :], func=AF.Silu,
                         scale_fn=lambda dc: scl[:, 4, dc:dc + 1], n_free=GRP,
                         pre_wait=(sa, sa.v))
                saG = sa.v

                stG0, svG0 = st.v, sv.v

                @block.tensor
                def _(tensor):
                    sa.wait(tensor, saG)
                    for i in range(4):
                        sv.wait(tensor, svG0 + i)
                        for dc in range(8):
                            ins = tensor.transpose(psTR[:, dc * 128:(dc + 1) * 128],
                                                   gT[:, dc, i * 128:(i + 1) * 128],
                                                   ident[:])
                        st.inc(ins)

                @block.vector
                def _(vector):
                    for i in range(4):
                        st.wait(vector, stG0 + i + 1)
                        sv.inc(vector.tensor_copy(
                            out=g_tok[:, i, :].rearrange("p (k c) -> p k c", k=8),
                            in_=psTR[:].rearrange("p (k c) -> p k c", k=8)))

                sdY = sd.v
                stY0, svY0, saY0 = st.v, sv.v, sa.v

                @block.sync
                def _(sync):
                    for i in range(4):
                        if i >= 2:
                            st.wait(sync, stY0 + 3 * (i - 2) + 1)
                        sd.inc(sync.dma_start(out=y_sb[:, i % 2, :],
                                              in_=y[t0 + i * 128: t0 + (i + 1) * 128, :]), 16)

                @block.tensor
                def _(tensor):
                    for i in range(4):
                        tensor.wait_ge(sd.h, sdY + 16 * min(i + 2, 4))
                        sv.wait(tensor, svY0 + i)
                        for dc in range(8):
                            ins = tensor.transpose(psTR[:, dc * 128:(dc + 1) * 128],
                                                   y_sb[:, i % 2, dc * 128:(dc + 1) * 128],
                                                   ident[:])
                        st.inc(ins)
                        sv.wait(tensor, svY0 + i + 1)
                        for oc in range(2):
                            if i >= 1:
                                sa.wait(tensor, saY0 + 2 * (i - 1) + oc + 1)
                            tensor.matmul(psS[oc][:, :], ones[:], bp_sb[:, oc * 512:(oc + 1) * 512],
                                          start=True, stop=False)
                            for k in range(8):
                                ins = tensor.matmul(psS[oc][:, :], yT[:, k, :],
                                                    WPsb[:, k, oc * 512:(oc + 1) * 512],
                                                    start=False, stop=(k == 7))
                            st.inc(ins)

                @block.vector
                def _(vector):
                    for i in range(4):
                        st.wait(vector, stY0 + 3 * i + 1)
                        sv.inc(vector.tensor_copy(
                            out=yT[:].rearrange("p k c -> p k c"),
                            in_=psTR[:].rearrange("p (k c) -> p k c", k=8)))

                @block.scalar
                def _(scalar):
                    for i in range(4):
                        for oc in range(2):
                            st.wait(scalar, stY0 + 3 * i + 2 + oc)
                            sa.inc(scalar.copy(out=s_tok[:, i, oc * 512:(oc + 1) * 512],
                                               in_=psS[oc][:, :]))

                if debug and g_i == 0:
                    dbg_sv, dbg_sa = sv.v, sa.v

                    @block.sync
                    def _(sync):
                        sync.wait_ge(sv.h, dbg_sv)
                        sync.wait_ge(sa.h, dbg_sa)
                        sd.inc(sync.dma_start(out=dbg_xn, in_=xn_bf[:]), 16)
                        sd.inc(sync.dma_start(out=dbg_g, in_=g_tok[:]), 16)
                        sd.inc(sync.dma_start(out=dbg_s, in_=s_tok[:]), 16)
                        sd.inc(sync.dma_start(out=dbg_h1, in_=hA[:]), 16)

                svD0, saD = sv.v, sa.v
                sdD = sd.v

                @block.vector
                def _(vector):
                    for i in range(4):
                        sa.wait(vector, saD)
                        if i >= 2:
                            vector.wait_ge(sd.h, sdD + 16 * i)
                        sv.wait(vector)
                        sv.inc(vector.tensor_tensor(out=tmp_f[:], in0=s_tok[:, i, :],
                                                    in1=g_tok[:, i, :], op=ALU.mult))
                        sv.wait(vector)
                        sv.inc(vector.tensor_tensor(out=x1t[:, i % 2, :], in0=x_g[:, i, :],
                                                    in1=tmp_f[:], op=ALU.add))

                @block.sync
                def _(sync):
                    for i in range(4):
                        sv.wait(sync, svD0 + 2 * i + 2)
                        sd.inc(sync.dma_start(out=x1[t0 + i * 128: t0 + (i + 1) * 128, :],
                                              in_=x1t[:, i % 2, :]), 16)

                resid_done[0] = sv.v

            @block.sync
            def _(sync):
                sd.wait(sync)

    return nc


def host_prep_B(inputs, b, j, yfull=None):
    x = inputs["L0_attn_x"][b, j * HALF:(j + 1) * HALF, :]
    y = (inputs["_yfull"] if yfull is None else yfull)[b, j * HALF:(j + 1) * HALF, :]
    return {
        "x": np.ascontiguousarray(x, dtype=np.float32),
        "y": np.ascontiguousarray(np.asarray(y).astype(BF)),
        "W1": prep_lhsT(inputs["L0_ffn_w1_w"].T),
        "W2": prep_lhsT(inputs["L0_ffn_w2_w"].T),
        "H1": prep_lhsT(inputs["L0_ffn_h1_w"].T),
        "H2": prep_lhsT(inputs["L0_ffn_h2_w"].T),
        "W3": prep_lhsT(inputs["L0_ffn_w3_w"].T),
        "S1": prep_pvec(inputs["L0_ffn_w1_s"]),
        "S2": prep_pvec(inputs["L0_ffn_w2_s"]),
        "SH1": prep_pvec(inputs["L0_ffn_h1_s"]),
        "SH2": prep_pvec(inputs["L0_ffn_h2_s"]),
        "S3": prep_pvec(inputs["L0_ffn_w3_s"]),
        "WP": prep_rhs(inputs["L0_attn_Wp"]),
        "BP": np.ascontiguousarray(inputs["L0_attn_bp"].reshape(1, D)),
    }






def build_A(upto=6, p1=4):
    nc = bass.Bass("TRN2", target_bir_lowering=False)
    NCH = 32
    CPAD = L + 8
    x = nc.dram_tensor("x", [L, D], F32, kind="ExternalInput").ap()
    xprev = nc.dram_tensor("xprev", [1, D], F32, kind="ExternalInput").ap()
    WQ = nc.dram_tensor("WQ", [4, 128, 8, 128], F32, kind="ExternalInput").ap()
    WK = nc.dram_tensor("WK", [4, 128, 8, 128], F32, kind="ExternalInput").ap()
    WZ = nc.dram_tensor("WZ", [4, 128, 8, 128], F32, kind="ExternalInput").ap()
    WV = nc.dram_tensor("WV", [8, 128, 512], F32, kind="ExternalInput").ap()
    WI = nc.dram_tensor("WI", [8, 128, 512], F32, kind="ExternalInput").ap()
    BQ = nc.dram_tensor("BQ", [128, 4], F32, kind="ExternalInput").ap()
    BK = nc.dram_tensor("BK", [128, 4], F32, kind="ExternalInput").ap()
    BZ = nc.dram_tensor("BZ", [128, 4], F32, kind="ExternalInput").ap()
    BVr = nc.dram_tensor("BVr", [1, 512], F32, kind="ExternalInput").ap()
    K2M = nc.dram_tensor("K2M", [128, 1], F32, kind="ExternalInput").ap()
    y = nc.dram_tensor("y", [L, 512], BF16, kind="ExternalOutput").ap()
    o1d = nc.dram_tensor("o1d", [L + 1, 512], F32, kind="Internal").ap()
    xnd = nc.dram_tensor("xnd", [L, 512], BF16, kind="Internal").ap()
    vtd = nc.dram_tensor("vtd", [L, 512], BF16, kind="Internal").ap()

    ident_c = nc.inline_tensor(np.eye(128).astype(BF), name="identc").ap()
    ones_c = nc.inline_tensor(np.ones((1, 128), dtype=BF), name="onesc").ap()
    trilm = np.tril(np.ones((128, 128), np.float32))
    masks_np = np.ascontiguousarray(
        np.stack([trilm, trilm.T, trilm * 0.125, trilm.T * 0.125], 1).astype(BF))
    masks_c = nc.inline_tensor(masks_np, name="masksc").ap()
    zrow_c = nc.inline_tensor(np.zeros((1, 512), np.float32), name="zrowc").ap()

    with ExitStack() as stack:
        e = stack.enter_context
        env = {"nc": nc}
        x_t = e(nc.sbuf_tensor("x_t", [128, 2, 1024], F32))
        sq = e(nc.sbuf_tensor("sq", [128, 1024], BF16))
        st8 = e(nc.sbuf_tensor("st8", [128, 8], F32))
        xn_bf = e(nc.sbuf_tensor("xn_bf", [128, 2, 1024], BF16))
        slab = e(nc.sbuf_tensor("slab", [128, 2, 8, 128], BF16))
        QT = e(nc.sbuf_tensor("QT", [128, 4, CPAD], BF16))
        KT = e(nc.sbuf_tensor("KT", [128, 4, CPAD], BF16))
        ZT = e(nc.sbuf_tensor("ZT", [128, 4, CPAD], BF16))
        WqSB = e(nc.sbuf_tensor("WqSB", [128, 4, 8, 128], BF16))
        WkSB = e(nc.sbuf_tensor("WkSB", [128, 4, 8, 128], BF16))
        WzSB = e(nc.sbuf_tensor("WzSB", [128, 4, 8, 128], BF16))
        WvSB = e(nc.sbuf_tensor("WvSB", [128, 8, 512], BF16))
        WiSB = e(nc.sbuf_tensor("WiSB", [128, 8, 512], BF16))
        bqSB = e(nc.sbuf_tensor("bqSB", [128, 4], F32))
        bkSB = e(nc.sbuf_tensor("bkSB", [128, 4], F32))
        bzSB = e(nc.sbuf_tensor("bzSB", [128, 4], F32))
        bzs = e(nc.sbuf_tensor("bzs", [128, 4], F32))
        bvRow = e(nc.sbuf_tensor("bvRow", [1, 512], BF16))
        k2m = e(nc.sbuf_tensor("k2m", [128, 1], F32))
        ident = e(nc.sbuf_tensor("ident", [128, 128], BF16))
        ones = e(nc.sbuf_tensor("ones", [1, 128], BF16))
        masksSB = e(nc.sbuf_tensor("masksSB", [128, 4, 128], BF16))
        zb = e(nc.sbuf_tensor("zb", [1, 512], F32))
        Vc = e(nc.sbuf_tensor("Vc", [128, 512], BF16))
        Ktc = e(nc.sbuf_tensor("Ktc", [128, 512], BF16))
        Am = e(nc.sbuf_tensor("Am", [128, 128], BF16))
        AmT = e(nc.sbuf_tensor("AmT", [128, 128], BF16))
        O1c = e(nc.sbuf_tensor("O1c", [128, 512], F32))
        S32 = e(nc.sbuf_tensor("S32", [128, 4, 64], F32))
        Sbf = e(nc.sbuf_tensor("Sbf", [128, 8, 64], BF16))
        W32 = e(nc.sbuf_tensor("W32", [128, 4, 64], F32))
        Wbf = e(nc.sbuf_tensor("Wbf", [128, 8, 64], BF16))
        xnE = e(nc.sbuf_tensor("xnE", [128, 512], BF16))
        O1e = e(nc.sbuf_tensor("O1e", [128, 512], F32))
        O1y = e(nc.sbuf_tensor("O1y", [128, 512], F32))
        E_c = e(nc.sbuf_tensor("E_c", [128, 512], BF16))
        q2T = e(nc.sbuf_tensor("q2T", [128, 4, 128], BF16))
        q2t = e(nc.sbuf_tensor("q2t", [128, 4, 128], F32))
        ycb = e(nc.sbuf_tensor("ycb", [128, 512], BF16))
        Vob = e(nc.sbuf_tensor("Vob", [128, 512], BF16))
        xnob = e(nc.sbuf_tensor("xnob", [128, 512], BF16))
        psMM = [e(nc.psum_tensor(f"psMM{i}", [128, 512], F32)) for i in range(2)]
        psTR = e(nc.psum_tensor("psTR", [128, 1024], BF16))
        psA = e(nc.psum_tensor("psA", [128, 128], F32))
        psAT = e(nc.psum_tensor("psAT", [128, 128], F32))
        psO = e(nc.psum_tensor("psO", [128, 64], F32))
        psDS = e(nc.psum_tensor("psDS", [128, 64], F32))
        psKT = e(nc.psum_tensor("psKT", [128, 512], BF16))

        sd = Sem(stack, nc, "sd")
        gw = Sem(stack, nc, "gw")
        st = Sem(stack, nc, "st")
        sv = Sem(stack, nc, "sv")
        sa = Sem(stack, nc, "sa")
        env.update(sq=sq[:], st8=st8[:], sd=sd, gw=gw, st=st, sv=sv, sa=sa)

        def hsl(t, h, cols):
            return t[(h % 2) * 64:(h % 2) * 64 + 64, h // 2, cols]

        def hst(t, h):
            return t[(h % 2) * 64:(h % 2) * 64 + 64, h // 2, :]

        def hps(h):
            return psDS[(h % 2) * 64:(h % 2) * 64 + 64, :]

        def ktc_h(h):
            c0 = (h // 2) * 128 + (h % 2) * 64
            return Ktc[:, c0:c0 + 64]

        def identb(h):
            return ident[0:64, 0:64] if h % 2 == 0 else ident[64:128, 64:128]

        def barrier(block):
            vals = (sd.v, gw.v, st.v, sv.v, sa.v)
            for mk in (block.sync, block.gpsimd, block.tensor, block.vector, block.scalar):
                def _f(eng, vals=vals):
                    eng.wait_ge(sd.h, vals[0]); eng.wait_ge(gw.h, vals[1])
                    eng.wait_ge(st.h, vals[2]); eng.wait_ge(sv.h, vals[3])
                    eng.wait_ge(sa.h, vals[4])
                mk(_f)

        with nc.Block() as block:
            @block.sync
            def _(sync):
                for dst, src in [(bqSB[:], BQ), (bkSB[:], BK), (bzSB[:], BZ),
                                 (k2m[:], K2M), (ident[:], ident_c),
                                 (ones[:], ones_c), (masksSB[:], masks_c),
                                 (zb[:], zrow_c)]:
                    sd.inc(sync.dma_start(out=dst, in_=src), 16)
                sync.wait_ge(sd.h, sd.v)
                sd.inc(sync.dma_start(out=o1d[0:1, :], in_=zb[:]), 16)
                sync.wait_ge(sd.h, sd.v)

            @block.gpsimd
            def _(g):
                for dst, src in [(WqSB[:], WQ), (WkSB[:], WK), (WzSB[:], WZ)]:
                    gw.inc(g.dma_start(out=dst, in_=src.rearrange("dc p k c -> p dc k c")), 16)
                gw.inc(g.dma_start(out=WvSB[:], in_=WV.rearrange("k p c -> p k c")), 16)
                gw.inc(g.dma_start(out=WiSB[:], in_=WI.rearrange("k p c -> p k c")), 16)
                gw.inc(g.dma_start(out=bvRow[:], in_=BVr), 16)
                g.wait_ge(gw.h, gw.v)

            barrier(block)

            @block.vector
            def _(vector):
                sv.inc(vector.tensor_scalar(out=bzs[:], in0=bzSB[:], scalar1=0.02 / 32.0,
                                            scalar2=None, op0=ALU.mult))
                vector.memset(S32[:], 0.0)
                vector.memset(Sbf[:], 0.0)
                vector.memset(W32[:], 0.0)
                sv.inc(vector.memset(Wbf[:], 0.0))

            barrier(block)

            if upto >= 1:
                projs = [(WqSB, QT, "q"), (WkSB, KT, "k"), (WzSB, ZT, "z")]
                NT1 = 33
                col0s = [j * 128 if j < 32 else L - 127 for j in range(NT1)]
                sdP, stP, svP, saP = sd.v, st.v, sv.v, sa.v
                sd_at = []
                acc = sdP
                for j in range(NT1):
                    acc += 32 if j == 0 else 16
                    sd_at.append(acc)

                @block.sync
                def _(sync):
                    for j in range(NT1):
                        c0 = col0s[j]
                        if j >= 2:
                            sv.wait(sync, svP + 5 * (j - 2) + 4)
                        if j == 0:
                            sync.dma_start(out=x_t[0:1, 0, :], in_=xprev).then_inc(sd.h, 16)
                            sync.dma_start(out=x_t[1:128, 0, :], in_=x[0:127, :]).then_inc(sd.h, 16)
                        else:
                            sync.dma_start(out=x_t[:, j % 2, :],
                                           in_=x[c0 - 1:c0 + 127, :]).then_inc(sd.h, 16)
                    sd.v = sd_at[-1]

                @block.vector
                def _(vector):
                    for j in range(NT1):
                        vector.wait_ge(sd.h, sd_at[min(j + 1, NT1 - 1)])
                        sv.inc(vector.tensor_reduce(out=st8[:, 0:1], in_=x_t[:, j % 2, :],
                                                    axis=AX.X, op=ALU.add))
                        sv.wait(vector)
                        sv.inc(vector.tensor_scalar(out=st8[:, 1:2], in0=st8[:, 0:1],
                                                    scalar1=-1.0 / D, scalar2=None,
                                                    op0=ALU.mult))
                        sv.wait(vector)
                        sa.wait(vector, saP + 15 * j + 1)
                        sv.inc(vector.tensor_scalar(out=st8[:, 3:4], in0=st8[:, 2:3],
                                                    scalar1=1.0 / D, scalar2=1e-5,
                                                    op0=ALU.mult, op1=ALU.add))
                        sa.wait(vector, saP + 15 * j + 3)
                        if j >= 2:
                            st.wait(vector, stP + 13 * (j - 2) + 1)
                        sv.inc(vector.tensor_scalar(out=xn_bf[:, j % 2, :],
                                                    in0=x_t[:, j % 2, :],
                                                    scalar1=st8[:, 1:2], scalar2=st8[:, 5:6],
                                                    op0=ALU.add, op1=ALU.mult))
                        st.wait(vector, stP + 13 * j + 1)
                        sv.inc(vector.tensor_copy(
                            out=slab[:, j % 2, :, :],
                            in_=psTR[:].rearrange("p (k c) -> p k c", k=8)))

                @block.scalar
                def _(scalar):
                    for j in range(NT1):
                        sv.wait(scalar, svP + 5 * j + 2)
                        sa.inc(scalar.activation(out=sq[:], in_=x_t[:, j % 2, :],
                                                 func=AF.Square, bias=st8[:, 1:2],
                                                 accum_out=st8[:, 2:3]))
                        sv.wait(scalar, svP + 5 * j + 3)
                        sa.inc(scalar.activation(out=st8[:, 4:5], in_=st8[:, 3:4],
                                                 func=AF.Ln))
                        sa.wait(scalar)
                        sa.inc(scalar.activation(out=st8[:, 5:6], in_=st8[:, 4:5],
                                                 func=AF.Exp, scale=-0.5))
                        cols = slice(col0s[j], col0s[j] + 128)
                        for gidx in range(12):
                            kind = projs[gidx // 4][2]
                            dc = gidx % 4
                            st.wait(scalar, stP + 13 * j + 2 + gidx)
                            ps = psMM[gidx % 2][:, :128]
                            if kind == "q":
                                ins = scalar.activation(out=QT[:, dc, cols], in_=ps,
                                                        func=AF.Identity,
                                                        bias=bqSB[:, dc:dc + 1])
                            elif kind == "k":
                                ins = scalar.activation(out=KT[:, dc, cols], in_=ps,
                                                        func=AF.Identity,
                                                        bias=bkSB[:, dc:dc + 1])
                            else:
                                ins = scalar.activation(out=ZT[:, dc, cols], in_=ps,
                                                        func=AF.Sigmoid, scale=0.02 / 32.0,
                                                        bias=bzs[:, dc:dc + 1])
                            sa.inc(ins)

                @block.tensor
                def _(tensor):
                    for j in range(NT1):
                        sv.wait(tensor, svP + 5 * j + 4)
                        if j >= 1:
                            sv.wait(tensor, svP + 5 * (j - 1) + 5)
                        for dc in range(8):
                            ins = tensor.transpose(psTR[:, dc * 128:(dc + 1) * 128],
                                                   xn_bf[:, j % 2, dc * 128:(dc + 1) * 128],
                                                   ident[:])
                        st.inc(ins)
                        sv.wait(tensor, svP + 5 * j + 5)
                        for gidx in range(12):
                            wsb = projs[gidx // 4][0]
                            dc = gidx % 4
                            G = 12 * j + gidx
                            if G >= 2:
                                jj, gg = divmod(G - 2, 12)
                                sa.wait(tensor, saP + 15 * jj + 4 + gg)
                            for k in range(8):
                                ins = tensor.matmul(psMM[gidx % 2][:, :128],
                                                    wsb[:, dc, k, :],
                                                    slab[:, j % 2, k, :],
                                                    start=(k == 0), stop=(k == 7))
                            st.inc(ins)

                assert sv.v == svP + 5 * NT1 and sa.v == saP + 15 * NT1
                assert st.v == stP + 13 * NT1

            if upto >= 2:
                barrier(block)

                @block.vector
                def _(vector):
                    sv.inc(vector.tensor_scalar(out=ZT[:, :, 0:1], in0=ZT[:, :, 0:1],
                                                scalar1=k2m[:, 0:1], scalar2=None,
                                                op0=ALU.mult))

                barrier(block)

            if upto >= 3:
                sdB, gwB, stB, svB, saB = sd.v, gw.v, st.v, sv.v, sa.v

                @block.sync
                def _(sync):
                    for t in range(32):
                        if t >= 2:
                            sv.wait(sync, svB + 5 * (t - 2) + 4)
                        sync.dma_start(out=x_t[:, t % 2, :],
                                       in_=x[t * 128:(t + 1) * 128, :]).then_inc(sd.h, 16)
                    sd.v = sdB + 32 * 16

                @block.vector
                def _(vector):
                    for t in range(32):
                        vector.wait_ge(sd.h, sdB + 16 * min(t + 2, 32))
                        sv.inc(vector.tensor_reduce(out=st8[:, 0:1], in_=x_t[:, t % 2, :],
                                                    axis=AX.X, op=ALU.add))
                        sv.wait(vector)
                        sv.inc(vector.tensor_scalar(out=st8[:, 1:2], in0=st8[:, 0:1],
                                                    scalar1=-1.0 / D, scalar2=None,
                                                    op0=ALU.mult))
                        sv.wait(vector)
                        sa.wait(vector, saB + 5 * t + 1)
                        sv.inc(vector.tensor_scalar(out=st8[:, 3:4], in0=st8[:, 2:3],
                                                    scalar1=1.0 / D, scalar2=1e-5,
                                                    op0=ALU.mult, op1=ALU.add))
                        sa.wait(vector, saB + 5 * t + 3)
                        if t >= 2:
                            st.wait(vector, stB + 3 * (t - 2) + 1)
                        sv.inc(vector.tensor_scalar(out=xn_bf[:, t % 2, :],
                                                    in0=x_t[:, t % 2, :],
                                                    scalar1=st8[:, 1:2], scalar2=st8[:, 5:6],
                                                    op0=ALU.add, op1=ALU.mult))
                        st.wait(vector, stB + 3 * t + 1)
                        sv.inc(vector.tensor_copy(
                            out=slab[:, t % 2, :, :],
                            in_=psTR[:].rearrange("p (k c) -> p k c", k=8)))

                @block.scalar
                def _(scalar):
                    for t in range(32):
                        sv.wait(scalar, svB + 5 * t + 2)
                        sa.inc(scalar.activation(out=sq[:], in_=x_t[:, t % 2, :],
                                                 func=AF.Square, bias=st8[:, 1:2],
                                                 accum_out=st8[:, 2:3]))
                        sv.wait(scalar, svB + 5 * t + 3)
                        sa.inc(scalar.activation(out=st8[:, 4:5], in_=st8[:, 3:4],
                                                 func=AF.Ln))
                        sa.wait(scalar)
                        sa.inc(scalar.activation(out=st8[:, 5:6], in_=st8[:, 4:5],
                                                 func=AF.Exp, scale=-0.5))
                        st.wait(scalar, stB + 3 * t + 2)
                        if t >= 1:
                            scalar.wait_ge(gw.h, gwB + 32 * (t - 1) + 32)
                        sa.inc(scalar.copy(out=Vob[:], in_=psMM[0][:, :512]))
                        st.wait(scalar, stB + 3 * t + 3)
                        if t >= 1:
                            scalar.wait_ge(gw.h, gwB + 32 * (t - 1) + 32)
                        sa.inc(scalar.copy(out=xnob[:], in_=psMM[1][:, :512]))

                @block.tensor
                def _(tensor):
                    for t in range(32):
                        sv.wait(tensor, svB + 5 * t + 4)
                        if t >= 1:
                            sv.wait(tensor, svB + 5 * (t - 1) + 5)
                        for dc in range(8):
                            ins = tensor.transpose(psTR[:, dc * 128:(dc + 1) * 128],
                                                   xn_bf[:, t % 2, dc * 128:(dc + 1) * 128],
                                                   ident[:])
                        st.inc(ins)
                        sv.wait(tensor, svB + 5 * t + 5)
                        if t >= 1:
                            sa.wait(tensor, saB + 5 * (t - 1) + 4)
                        tensor.matmul(psMM[0][:, :512], ones[:], bvRow[:],
                                      start=True, stop=False)
                        for k in range(8):
                            ins = tensor.matmul(psMM[0][:, :512], slab[:, t % 2, k, :],
                                                WvSB[:, k, :], start=False, stop=(k == 7))
                        st.inc(ins)
                        if t >= 1:
                            sa.wait(tensor, saB + 5 * (t - 1) + 5)
                        for k in range(8):
                            ins = tensor.matmul(psMM[1][:, :512], slab[:, t % 2, k, :],
                                                WiSB[:, k, :], start=(k == 0), stop=(k == 7))
                        st.inc(ins)

                @block.gpsimd
                def _(g):
                    for t in range(32):
                        sa.wait(g, saB + 5 * t + 4)
                        g.dma_start(out=vtd[t * 128:(t + 1) * 128, :],
                                    in_=Vob[:]).then_inc(gw.h, 16)
                        sa.wait(g, saB + 5 * t + 5)
                        g.dma_start(out=xnd[t * 128:(t + 1) * 128, :],
                                    in_=xnob[:]).then_inc(gw.h, 16)
                    gw.v = gwB + 32 * 32

                barrier(block)

            if upto >= 4:
                sd1, st1, sv1, sa1 = sd.v, st.v, sv.v, sa.v

                @block.sync
                def _(sync):
                    for c in range(NCH):
                        if c >= 1:
                            st.wait(sync, st1 + 25 * c)
                        sync.dma_start(out=Vc[:], in_=vtd[c * 128:(c + 1) * 128, :]).then_inc(sd.h, 16)
                        sa.wait(sync, sa1 + 16 * c + 15)
                        sync.dma_start(out=o1d[c * 128 + 1:c * 128 + 129, :],
                                       in_=O1c[:]).then_inc(sd.h, 16)
                    sd.v = sd1 + 32 * NCH

                @block.tensor
                def _(tensor):
                    for c in range(NCH):
                        tensor.wait_ge(sa.h, sa1 + 16 * c)
                        if c >= 1:
                            sv.wait(tensor, sv1 + 17 * (c - 1) + 1)
                        if p1 >= 1:
                            for dc in range(4):
                                ins = tensor.transpose(psKT[:, dc * 128:(dc + 1) * 128],
                                                       KT[:, dc, c * 128 + 1:c * 128 + 129],
                                                       ident[:])
                        else:
                            ins = tensor.nop()
                        st.inc(ins)
                        tensor.wait_ge(sd.h, sd1 + 32 * c + 16)
                        for h in range(8):
                            cols = slice(c * 128 + 1, c * 128 + 129)
                            if h >= 1:
                                sv.wait(tensor, sv1 + 17 * c + 2 * h)
                            else:
                                sv.wait(tensor, sv1 + 17 * c)
                            if p1 >= 2:
                                tensor.matmul(psA[:], hsl(QT, h, cols), hsl(KT, h, cols),
                                              start=True, stop=True)
                                ins = tensor.matmul(psAT[:], hsl(KT, h, cols), hsl(QT, h, cols),
                                                    start=True, stop=True)
                            else:
                                ins = tensor.nop()
                            st.inc(ins)
                            sv.wait(tensor, sv1 + 17 * c + 2 * h + 2)
                            if h >= 1:
                                sa.wait(tensor, sa1 + 16 * c + 2 * (h - 1) + 1)
                            if p1 >= 3:
                                tensor.matmul(psO[:], AmT[:], Vc[:, h * 64:(h + 1) * 64],
                                              start=True, stop=False)
                                ins = tensor.matmul(psO[:], QT[:, h // 2, cols],
                                                    Sbf[:, h, :], start=False, stop=True)
                            else:
                                ins = tensor.nop()
                            st.inc(ins)
                            if h >= 1:
                                sv.wait(tensor, sv1 + 17 * c + 2 * h + 1)
                            if p1 >= 4:
                                ins = tensor.matmul(hps(h), ktc_h(h),
                                                    Vc[:, h * 64:(h + 1) * 64],
                                                    start=True, stop=True,
                                                    tile_position=(0, (h % 2) * 64))
                            else:
                                ins = tensor.nop()
                            st.inc(ins)

                @block.vector
                def _(vector):
                    for c in range(NCH):
                        st.wait(vector, st1 + 25 * c + 1)
                        if p1 >= 1:
                            sv.inc(vector.tensor_copy(out=Ktc[:], in_=psKT[:]))
                        else:
                            sv.inc(vector.engine_nop())
                        for h in range(8):
                            st.wait(vector, st1 + 25 * c + 3 * h + 2)
                            if p1 >= 2:
                                vector.tensor_tensor(out=Am[:], in0=psA[:],
                                                     in1=masksSB[:, 0, :], op=ALU.mult)
                                sv.inc(vector.tensor_tensor(out=AmT[:], in0=psAT[:],
                                                            in1=masksSB[:, 1, :], op=ALU.mult))
                            else:
                                sv.inc(vector.engine_nop())
                            st.wait(vector, st1 + 25 * c + 3 * h + 4)
                            if p1 >= 4:
                                sv.inc(vector.tensor_tensor(out=hst(S32, h), in0=hst(S32, h),
                                                            in1=hps(h), op=ALU.add))
                            else:
                                sv.inc(vector.engine_nop())

                @block.scalar
                def _(scalar):
                    for c in range(NCH):
                        for h in range(8):
                            st.wait(scalar, st1 + 25 * c + 3 * h + 3)
                            if h == 0 and c >= 1:
                                scalar.wait_ge(sd.h, sd1 + 32 * c + 16)
                            if p1 >= 3:
                                sa.inc(scalar.copy(out=O1c[:, h * 64:(h + 1) * 64], in_=psO[:]))
                            else:
                                sa.inc(scalar.nop())
                            sv.wait(scalar, sv1 + 17 * c + 2 * h + 3)
                            if p1 >= 4:
                                sa.inc(scalar.copy(out=Sbf[(h % 2) * 64:(h % 2) * 64 + 64, h, :],
                                                   in_=hst(S32, h)))
                            else:
                                sa.inc(scalar.nop())

                barrier(block)

            if upto >= 5:
                sd2, st2, sv2, sa2 = sd.v, st.v, sv.v, sa.v

                def sd_loads_done(c):
                    return sd2 + 16 * (3 + 4 * c)

                @block.sync
                def _(sync):
                    for c in range(NCH):
                        if c >= 1:
                            sv.wait(sync, sv2 + 28 * (c - 1) + 28)
                            sync.dma_start(out=y[(c - 1) * 128:c * 128, :],
                                           in_=ycb[:]).then_inc(sd.h, 16)
                        r0 = c * 128
                        sync.dma_start(out=xnE[:], in_=xnd[r0:r0 + 128, :]).then_inc(sd.h, 16)
                        sync.dma_start(out=O1e[:], in_=o1d[r0:r0 + 128, :]).then_inc(sd.h, 16)
                        sync.dma_start(out=O1y[:], in_=o1d[r0 + 1:r0 + 129, :]).then_inc(sd.h, 16)
                    sv.wait(sync, sv2 + 28 * NCH)
                    sync.dma_start(out=y[(NCH - 1) * 128:NCH * 128, :],
                                   in_=ycb[:]).then_inc(sd.h, 16)
                    sd.v = sd2 + 16 * (4 * NCH)

                @block.tensor
                def _(tensor):
                    for c in range(NCH):
                        tensor.wait_ge(sa.h, sa2 + 8 * c)
                        if c >= 1:
                            sv.wait(tensor, sv2 + 28 * (c - 1) + 1)
                        for dc in range(4):
                            ins = tensor.transpose(psKT[:, dc * 128:(dc + 1) * 128],
                                                   ZT[:, dc, c * 128:c * 128 + 128],
                                                   ident[:])
                        st.inc(ins)
                        for h in range(8):
                            cols = slice(c * 128, c * 128 + 128)
                            if h == 0:
                                sv.wait(tensor, sv2 + 28 * c + 4)
                            else:
                                sv.wait(tensor, sv2 + 28 * c + 3 * h + 2)
                            tensor.matmul(psA[:], hsl(q2T, h, slice(0, 128)),
                                          hsl(ZT, h, cols), start=True, stop=True)
                            ins = tensor.matmul(psAT[:], hsl(ZT, h, cols),
                                                hsl(q2T, h, slice(0, 128)),
                                                start=True, stop=True)
                            st.inc(ins)
                            sv.wait(tensor, sv2 + 28 * c + 3 * h + 5)
                            tensor.matmul(psO[:], AmT[:], E_c[:, h * 64:(h + 1) * 64],
                                          start=True, stop=False)
                            ins = tensor.matmul(psO[:], q2T[:, h // 2, :],
                                                Wbf[:, h, :], start=False, stop=True)
                            st.inc(ins)
                            if h >= 1:
                                sv.wait(tensor, sv2 + 28 * c + 3 * h + 4)
                            ins = tensor.matmul(hps(h), ktc_h(h),
                                                E_c[:, h * 64:(h + 1) * 64],
                                                start=True, stop=True,
                                                tile_position=(0, (h % 2) * 64))
                            st.inc(ins)

                @block.vector
                def _(vector):
                    for c in range(NCH):
                        st.wait(vector, st2 + 25 * c + 1)
                        sv.inc(vector.tensor_copy(out=Ktc[:], in_=psKT[:]))
                        vector.wait_ge(sd.h, sd_loads_done(c))
                        sv.inc(vector.tensor_tensor(out=E_c[:], in0=xnE[:], in1=O1e[:],
                                                    op=ALU.subtract))
                        cols = slice(c * 128, c * 128 + 128)
                        sv.inc(vector.tensor_scalar(out=q2t[:], in0=QT[:, :, cols],
                                                    scalar1=0.0, scalar2=0.02,
                                                    op0=ALU.max, op1=ALU.mult))
                        sv.wait(vector)
                        sv.inc(vector.scalar_tensor_tensor(out=q2T[:], in0=QT[:, :, cols],
                                                           scalar=0.0, in1=q2t[:],
                                                           op0=ALU.min, op1=ALU.add))
                        for h in range(8):
                            st.wait(vector, st2 + 25 * c + 3 * h + 2)
                            vector.tensor_tensor(out=Am[:], in0=psA[:],
                                                 in1=masksSB[:, 2, :], op=ALU.mult)
                            sv.inc(vector.tensor_tensor(out=AmT[:], in0=psAT[:],
                                                        in1=masksSB[:, 3, :],
                                                        op=ALU.mult))
                            st.wait(vector, st2 + 25 * c + 3 * h + 3)
                            if h == 0 and c >= 1:
                                vector.wait_ge(sd.h, sd_loads_done(c))
                            sv.inc(vector.tensor_tensor(out=ycb[:, h * 64:(h + 1) * 64],
                                                        in0=psO[:],
                                                        in1=O1y[:, h * 64:(h + 1) * 64],
                                                        op=ALU.add))
                            st.wait(vector, st2 + 25 * c + 3 * h + 4)
                            sv.inc(vector.tensor_tensor(out=hst(W32, h), in0=hst(W32, h),
                                                        in1=hps(h), op=ALU.add))

                @block.scalar
                def _(scalar):
                    for c in range(NCH):
                        for h in range(8):
                            sv.wait(scalar, sv2 + 28 * c + 3 * h + 7)
                            sa.inc(scalar.activation(
                                out=Wbf[(h % 2) * 64:(h % 2) * 64 + 64, h, :],
                                in_=hst(W32, h), func=AF.Copy, scale=0.125))

            @block.sync
            def _(sync):
                sd.wait(sync)
                sync.wait_ge(gw.h, gw.v)

    return nc


def host_prep_A(inputs, b, hg):
    cols = slice(hg * 512, (hg + 1) * 512)
    return {
        "x": np.ascontiguousarray(inputs["L0_attn_x"][b], dtype=np.float32),
        "xprev": np.zeros((1, D), np.float32),
        "WQ": prep_lhsT(np.ascontiguousarray(inputs["L0_attn_Wq"][:, cols])),
        "WK": prep_lhsT(np.ascontiguousarray(inputs["L0_attn_Wk1"][:, cols])),
        "WZ": prep_lhsT(np.ascontiguousarray(inputs["L0_attn_Wk2"][:, cols])),
        "WV": prep_rhs(np.ascontiguousarray(inputs["L0_attn_Wv"][:, cols])),
        "WI": prep_rhs(np.ascontiguousarray(np.eye(D, dtype=np.float32)[:, cols])),
        "BQ": prep_pvec(np.ascontiguousarray(inputs["L0_attn_bq"][cols])),
        "BK": prep_pvec(np.ascontiguousarray(inputs["L0_attn_bk1"][cols])),
        "BZ": prep_pvec(np.ascontiguousarray(inputs["L0_attn_bk2"][cols])),
        "BVr": np.ascontiguousarray(inputs["L0_attn_bv"][cols].reshape(1, 512)),
        "K2M": np.zeros((128, 1), np.float32),
    }






def build_D():
    GRP = 512
    NG = HALF // GRP
    NE = 8
    nc = bass.Bass("TRN2", target_bir_lowering=False)
    x1 = nc.dram_tensor("x1", [HALF, D], F32, kind="ExternalInput").ap()
    ctx = nc.dram_tensor("ctx", [HALF, D], BF16, kind="ExternalInput").ap()
    WO = nc.dram_tensor("WO", [8, 128, 1024], F32, kind="ExternalInput").ap()
    BO = nc.dram_tensor("BO", [1, 1024], F32, kind="ExternalInput").ap()
    W1E = nc.dram_tensor("W1E", [NE, 32, 128, 8, 128], F32, kind="ExternalInput").ap()
    B1E = nc.dram_tensor("B1E", [NE, 128, 32], F32, kind="ExternalInput").ap()
    W2E = nc.dram_tensor("W2E", [NE, 32, 128, 1024], F32, kind="ExternalInput").ap()
    B2E = nc.dram_tensor("B2E", [NE, 1, 1024], F32, kind="ExternalInput").ap()
    WR = nc.dram_tensor("WR", [8, 128, 8], F32, kind="ExternalInput").ap()
    BR = nc.dram_tensor("BR", [1, 8], F32, kind="ExternalInput").ap()
    out = nc.dram_tensor("out", [HALF, D], F32, kind="ExternalOutput").ap()

    ident_c = nc.inline_tensor(np.eye(128).astype(BF), name="identc").ap()
    ones_c = nc.inline_tensor(np.ones((1, 128), dtype=BF), name="onesc").ap()

    with ExitStack() as stack:
        e = stack.enter_context
        env = {"nc": nc}
        x_g = e(nc.sbuf_tensor("x_g", [128, 4, 1024], F32))
        sq = e(nc.sbuf_tensor("sq", [128, 1024], BF16))
        st8 = e(nc.sbuf_tensor("st8", [128, 8], F32))
        xn_bf = e(nc.sbuf_tensor("xn_bf", [128, 4, 1024], BF16))
        xnT = e(nc.sbuf_tensor("xnT", [128, 8, GRP], BF16))
        hA = e(nc.sbuf_tensor("hA", [128, 32, GRP], BF16))
        out_acc = e(nc.sbuf_tensor("out_acc", [128, 4, 1024], F32))
        g_tok = e(nc.sbuf_tensor("g_tok", [128, 4, 1024], BF16))
        s_tok = e(nc.sbuf_tensor("s_tok", [128, 4, 1024], F32))
        wgt = e(nc.sbuf_tensor("wgt", [128, 4, 8], F32))
        r8 = e(nc.sbuf_tensor("r8", [128, 8], F32))
        rstat = e(nc.sbuf_tensor("rstat", [128, 4], F32))
        ctx_sb = e(nc.sbuf_tensor("ctx_sb", [128, 2, 1024], BF16))
        cT = e(nc.sbuf_tensor("cT", [128, 8, 128], BF16))
        wslab = [e(nc.sbuf_tensor(f"wslab{i}", [128, 1024], BF16)) for i in range(2)]
        wsl2 = [e(nc.sbuf_tensor(f"wsl2{i}", [128, 512], BF16)) for i in range(2)]
        WoSB = e(nc.sbuf_tensor("WoSB", [128, 8, 1024], BF16))
        WrSB = e(nc.sbuf_tensor("WrSB", [128, 8, 8], BF16))
        boR = e(nc.sbuf_tensor("boR", [1, 1024], BF16))
        brR = e(nc.sbuf_tensor("brR", [1, 8], BF16))
        b1SB = e(nc.sbuf_tensor("b1SB", [128, 32], F32))
        b2R = e(nc.sbuf_tensor("b2R", [1, 1024], BF16))
        ident = e(nc.sbuf_tensor("ident", [128, 128], BF16))
        ones = e(nc.sbuf_tensor("ones", [1, 128], BF16))
        x2t = e(nc.sbuf_tensor("x2t", [128, 2, 1024], F32))
        tmp_f = e(nc.sbuf_tensor("tmp_f", [128, 1024], F32))
        psMM = [e(nc.psum_tensor(f"psMM{i}", [128, 512], F32)) for i in range(2)]
        psTR = e(nc.psum_tensor("psTR", [128, 1024], BF16))
        psD = [e(nc.psum_tensor(f"psD{i}", [128, 512], F32)) for i in range(4)]
        psR = e(nc.psum_tensor("psR", [128, 8], F32))

        sd = Sem(stack, nc, "sd")
        gw = Sem(stack, nc, "gw")
        st = Sem(stack, nc, "st")
        sv = Sem(stack, nc, "sv")
        sa = Sem(stack, nc, "sa")
        gws = [Sem(stack, nc, "gws0"), Sem(stack, nc, "gws1")]
        gw2 = [Sem(stack, nc, "gw20"), Sem(stack, nc, "gw21")]
        env.update(sq=sq[:], st8=st8[:], sd=sd, gw=gw, st=st, sv=sv, sa=sa,
                   wslab=[w[:] for w in wslab], psMM=[p[:] for p in psMM], gws=gws)

        with nc.Block() as block:
            @block.sync
            def _(sync):
                for dst, src_ in [(ident[:], ident_c), (ones[:], ones_c)]:
                    sd.inc(sync.dma_start(out=dst, in_=src_), 16)
                sync.wait_ge(sd.h, sd.v)

            @block.gpsimd
            def _(g):
                gw.inc(g.dma_start(out=WoSB[:], in_=WO.rearrange("k p c -> p k c")), 16)
                gw.inc(g.dma_start(out=WrSB[:], in_=WR.rearrange("k p c -> p k c")), 16)
                gw.inc(g.dma_start(out=boR[:], in_=BO), 16)
                gw.inc(g.dma_start(out=brR[:], in_=BR), 16)
                g.wait_ge(gw.h, gw.v)

            pre_sd, pre_gw = sd.v, gw.v

            @block.tensor
            def _(tensor):
                tensor.wait_ge(sd.h, pre_sd)
                tensor.wait_ge(gw.h, pre_gw)

            @block.vector
            def _(vector):
                vector.wait_ge(sd.h, pre_sd)
                vector.wait_ge(gw.h, pre_gw)

            @block.scalar
            def _(scalar):
                scalar.wait_ge(sd.h, pre_sd)
                scalar.wait_ge(gw.h, pre_gw)

            resid_done = [0]
            for g_i in range(NG):
                t0 = g_i * GRP
                sdA = sd.v
                rd_prev = resid_done[0]

                @block.sync
                def _(sync):
                    if g_i > 0:
                        sv.wait(sync, rd_prev)
                    for i in range(4):
                        sync.dma_start(out=x_g[:, i, :],
                                       in_=x1[t0 + i * 128:t0 + (i + 1) * 128, :]).then_inc(sd.h, 16)
                    sync.wait_ge(sd.h, sdA + 64)
                sd.v = sdA + 64

                svA = sv.v
                stT0 = st.v
                emit_ln(block, env, n=4, x_fn=lambda i: x_g[:, i, :],
                        out_fn=lambda i: xn_bf[:, i, :],
                        vec_pre=lambda vector, i: vector.wait_ge(sd.h, sdA + 64))
                svT0 = sv.v

                @block.tensor
                def _(tensor):
                    for i in range(4):
                        sv.wait(tensor, svA + 4 * i + 4)
                        if i >= 1:
                            sv.wait(tensor, svT0 + i)
                        for dc in range(8):
                            ins = tensor.transpose(psTR[:, dc * 128:(dc + 1) * 128],
                                                   xn_bf[:, i, dc * 128:(dc + 1) * 128],
                                                   ident[:])
                        st.inc(ins)

                @block.vector
                def _(vector):
                    for i in range(4):
                        st.wait(vector, stT0 + i + 1)
                        sv.inc(vector.tensor_copy(
                            out=xnT[:, :, i * 128:(i + 1) * 128],
                            in_=psTR[:].rearrange("p (k c) -> p k c", k=8)))
                svXNT = sv.v

                stR0, svR0, saR0 = st.v, sv.v, sa.v

                @block.tensor
                def _(tensor):
                    sv.wait(tensor, svXNT)
                    for i in range(4):
                        tensor.matmul(psR[:], ones[:], brR[:], start=True, stop=False)
                        for k in range(8):
                            ins = tensor.matmul(psR[:], xnT[:, k, i * 128:(i + 1) * 128],
                                                WrSB[:, k, :], start=False, stop=(k == 7))
                        st.inc(ins)
                        sv.wait(tensor, svR0 + 5 * (i + 1))

                @block.vector
                def _(vector):
                    for i in range(4):
                        st.wait(vector, stR0 + i + 1)
                        sv.inc(vector.max(out=r8[:], in_=psR[:]))
                        sv.wait(vector)
                        sv.inc(vector.tensor_scalar(out=rstat[:, 0:1], in0=r8[:, 0:1],
                                                    scalar1=-1.0, scalar2=None,
                                                    op0=ALU.mult))
                        sa.wait(vector, saR0 + i + 1)
                        sv.inc(vector.scalar_tensor_tensor(out=x2t[:, 0, 0:8], in0=psR[:],
                                                           scalar=r8[:, 1:2], in1=sq[:, 0:8],
                                                           op0=ALU.is_ge, op1=ALU.mult,
                                                           accum_out=rstat[:, 1:2]))
                        sv.wait(vector)
                        sv.inc(vector.reciprocal(out=rstat[:, 2:3], in_=rstat[:, 1:2]))
                        sv.wait(vector)
                        sv.inc(vector.tensor_scalar(out=wgt[:, i, :], in0=x2t[:, 0, 0:8],
                                                    scalar1=rstat[:, 2:3], scalar2=None,
                                                    op0=ALU.mult))

                @block.scalar
                def _(scalar):
                    for i in range(4):
                        sv.wait(scalar, svR0 + 5 * i + 2)
                        sa.inc(scalar.activation(out=sq[:, 0:8], in_=psR[:], func=AF.Exp,
                                                 bias=rstat[:, 0:1]))

                svRt = sv.v

                for e_i in range(NE):
                    sdE = sd.v

                    gwE = gw.v

                    saPrevE = sa.v

                    @block.sync
                    def _(sync):
                        if e_i > 0:
                            sync.wait_ge(sa.h, saPrevE)
                        sync.dma_start(out=b1SB[:], in_=B1E[e_i]).then_inc(sd.h, 16)
                        sync.wait_ge(sd.h, sdE + 16)
                    sd.v = sdE + 16

                    @block.gpsimd
                    def _(g):
                        g.dma_start(out=b2R[:], in_=B2E[e_i]).then_inc(gw.h, 16)
                        g.wait_ge(gw.h, gwE + 16)
                    gw.v = gwE + 16

                    @block.tensor
                    def _(tensor):
                        tensor.wait_ge(sd.h, sdE + 16)
                        tensor.wait_ge(gw.h, gwE + 16)
                        if e_i == 0:
                            sv.wait(tensor, svRt)

                    mm_layer(block, env, wdram=W1E[e_i], n_dc=32, n_k=8,
                             rhs_fn=lambda k: xnT[:, k, :],
                             out_fn=lambda dc: hA[:, dc, :], func=AF.Gelu,
                             scale_fn=None, n_free=GRP,
                             bias_fn=lambda dc: b1SB[:, dc:dc + 1])
                    saH = sa.v

                    stD, svD, saD2 = st.v, sv.v, sa.v
                    w2target = [gw2[kk % 2].v + 16 * (kk // 2 + 1) for kk in range(64)]

                    @block.gpsimd
                    def _(g):
                        for oc in range(2):
                            for k in range(32):
                                kk = oc * 32 + k
                                if kk >= 2:
                                    tgt = kk - 1 if kk <= 32 else (33 if kk == 33 else kk)
                                    st.wait(g, stD + tgt)
                                gw2[kk % 2].inc(
                                    g.dma_start(out=wsl2[kk % 2][:],
                                                in_=W2E[e_i, k, :, oc * 512:(oc + 1) * 512]),
                                    16)

                    @block.tensor
                    def _(tensor):
                        sa.wait(tensor, saH)
                        sv.wait(tensor, svD)
                        for oc in range(2):
                            if oc == 1:
                                sv.wait(tensor, svD + 4)
                            tensor.matmul(psD[0][:], ones[:], b2R[:, oc * 512:(oc + 1) * 512],
                                          start=True, stop=False)
                            tensor.matmul(psD[1][:], ones[:], b2R[:, oc * 512:(oc + 1) * 512],
                                          start=True, stop=False)
                            tensor.matmul(psD[2][:], ones[:], b2R[:, oc * 512:(oc + 1) * 512],
                                          start=True, stop=False)
                            tensor.matmul(psD[3][:], ones[:], b2R[:, oc * 512:(oc + 1) * 512],
                                          start=True, stop=False)
                            for k in range(32):
                                kk = oc * 32 + k
                                tensor.wait_ge(gw2[kk % 2].h, w2target[kk])
                                for tt in range(4):
                                    ins = tensor.matmul(psD[tt][:],
                                                        hA[:, k, tt * 128:(tt + 1) * 128],
                                                        wsl2[kk % 2][:],
                                                        start=False, stop=(k == 31))
                                st.inc(ins, 2 if k == 31 else 1)

                    @block.vector
                    def _(vector):
                        for oc in range(2):
                            st.wait(vector, stD + 33 * (oc + 1))
                            for tt in range(4):
                                if e_i == 0:
                                    ins = vector.tensor_scalar(
                                        out=out_acc[:, tt, oc * 512:(oc + 1) * 512],
                                        in0=psD[tt][:], scalar1=wgt[:, tt, e_i:e_i + 1],
                                        scalar2=None, op0=ALU.mult)
                                else:
                                    ins = vector.scalar_tensor_tensor(
                                        out=out_acc[:, tt, oc * 512:(oc + 1) * 512],
                                        in0=psD[tt][:], scalar=wgt[:, tt, e_i:e_i + 1],
                                        in1=out_acc[:, tt, oc * 512:(oc + 1) * 512],
                                        op0=ALU.mult, op1=ALU.add)
                                sv.inc(ins)

                svMoE = sv.v

                saG0 = sa.v

                @block.scalar
                def _(scalar):
                    sv.wait(scalar, svMoE)
                    for i in range(4):
                        sa.inc(scalar.activation(out=g_tok[:, i, :], in_=out_acc[:, i, :],
                                                 func=AF.Silu))
                saG = sa.v

                sdY = sd.v
                stY0, svY0, saY0 = st.v, sv.v, sa.v

                @block.sync
                def _(sync):
                    for i in range(4):
                        if i >= 2:
                            st.wait(sync, stY0 + 3 * (i - 2) + 1)
                        sync.dma_start(out=ctx_sb[:, i % 2, :],
                                       in_=ctx[t0 + i * 128:t0 + (i + 1) * 128, :]
                                       ).then_inc(sd.h, 16)
                sd.v = sdY + 64

                @block.tensor
                def _(tensor):
                    for i in range(4):
                        tensor.wait_ge(sd.h, sdY + 16 * min(i + 2, 4))
                        sv.wait(tensor, svY0 + i)
                        for dc in range(8):
                            ins = tensor.transpose(psTR[:, dc * 128:(dc + 1) * 128],
                                                   ctx_sb[:, i % 2, dc * 128:(dc + 1) * 128],
                                                   ident[:])
                        st.inc(ins)
                        sv.wait(tensor, svY0 + i + 1)
                        for oc in range(2):
                            if i >= 1:
                                sa.wait(tensor, saY0 + 2 * (i - 1) + oc + 1)
                            tensor.matmul(psMM[oc][:, :512], ones[:],
                                          boR[:, oc * 512:(oc + 1) * 512],
                                          start=True, stop=False)
                            for k in range(8):
                                ins = tensor.matmul(psMM[oc][:, :512], cT[:, k, :],
                                                    WoSB[:, k, oc * 512:(oc + 1) * 512],
                                                    start=False, stop=(k == 7))
                            st.inc(ins)

                @block.vector
                def _(vector):
                    for i in range(4):
                        st.wait(vector, stY0 + 3 * i + 1)
                        sv.inc(vector.tensor_copy(
                            out=cT[:],
                            in_=psTR[:].rearrange("p (k c) -> p k c", k=8)))

                @block.scalar
                def _(scalar):
                    for i in range(4):
                        for oc in range(2):
                            st.wait(scalar, stY0 + 3 * i + 2 + oc)
                            sa.inc(scalar.copy(out=s_tok[:, i, oc * 512:(oc + 1) * 512],
                                               in_=psMM[oc][:, :512]))

                svD0 = sv.v
                saD = sa.v
                sdD = sd.v

                @block.vector
                def _(vector):
                    sa.wait(vector, saD)
                    for i in range(4):
                        if i >= 2:
                            vector.wait_ge(sd.h, sdD + 16 * i)
                        sv.wait(vector)
                        sv.inc(vector.tensor_tensor(out=tmp_f[:], in0=s_tok[:, i, :],
                                                    in1=g_tok[:, i, :], op=ALU.mult))
                        sv.wait(vector)
                        sv.inc(vector.tensor_tensor(out=x2t[:, i % 2, :], in0=x_g[:, i, :],
                                                    in1=tmp_f[:], op=ALU.add))

                @block.sync
                def _(sync):
                    for i in range(4):
                        sv.wait(sync, svD0 + 2 * i + 2)
                        sync.dma_start(out=out[t0 + i * 128:t0 + (i + 1) * 128, :],
                                       in_=x2t[:, i % 2, :]).then_inc(sd.h, 16)
                sd.v = sdD + 64

                resid_done[0] = sv.v

            @block.sync
            def _(sync):
                sd.wait(sync)
                sync.wait_ge(gw.h, gw.v)

    return nc


def host_prep_D(inputs, x1_full, ctx_full, b, j):
    rows = slice(j * HALF, (j + 1) * HALF)
    return {
        "x1": np.ascontiguousarray(x1_full[b][rows], dtype=np.float32),
        "ctx": np.ascontiguousarray(ctx_full[b][rows]),
        "WO": prep_rhs(np.ascontiguousarray(inputs["L1_attn_Wo"].T)),
        "BO": np.ascontiguousarray(inputs["L1_attn_bo"].reshape(1, D)),
        "W1E": np.stack([prep_lhsT(inputs["L1_ffn_W1"][e_]) for e_ in range(8)]),
        "B1E": np.stack([prep_pvec(inputs["L1_ffn_b1"][e_]) for e_ in range(8)]),
        "W2E": np.stack([prep_rhs(inputs["L1_ffn_W2"][e_]) for e_ in range(8)]),
        "B2E": np.ascontiguousarray(inputs["L1_ffn_b2"].reshape(8, 1, D)),
        "WR": prep_rhs(np.ascontiguousarray(inputs["L1_ffn_Wr"])),
        "BR": np.ascontiguousarray(inputs["L1_ffn_br"].reshape(1, 8)),
    }






def build_C():
    nc = bass.Bass("TRN2", target_bir_lowering=False)
    NT = 32
    NEG = -30000.0
    x1 = nc.dram_tensor("x1", [L, D], F32, kind="ExternalInput").ap()
    WQ = nc.dram_tensor("WQ", [4, 128, 8, 128], F32, kind="ExternalInput").ap()
    WK = nc.dram_tensor("WK", [4, 128, 8, 128], F32, kind="ExternalInput").ap()
    WV = nc.dram_tensor("WV", [8, 128, 512], F32, kind="ExternalInput").ap()
    BQ = nc.dram_tensor("BQ", [128, 4], F32, kind="ExternalInput").ap()
    BK = nc.dram_tensor("BK", [128, 4], F32, kind="ExternalInput").ap()
    BVr = nc.dram_tensor("BVr", [1, 512], F32, kind="ExternalInput").ap()
    ctxo = nc.dram_tensor("ctxo", [L, 512], BF16, kind="ExternalOutput").ap()

    ident_c = nc.inline_tensor(np.eye(128).astype(BF), name="identc").ap()
    ones_c = nc.inline_tensor(np.ones((1, 128), dtype=BF), name="onesc").ap()

    with ExitStack() as stack:
        e = stack.enter_context
        x_t = e(nc.sbuf_tensor("x_t", [128, 2, 1024], F32))
        sq = e(nc.sbuf_tensor("sq", [128, 1024], BF16))
        st8 = e(nc.sbuf_tensor("st8", [128, 8], F32))
        xn_bf = e(nc.sbuf_tensor("xn_bf", [128, 2, 1024], BF16))
        slab = e(nc.sbuf_tensor("slab", [128, 2, 8, 128], BF16))
        QT = e(nc.sbuf_tensor("QT", [128, 4, L], BF16))
        KT = e(nc.sbuf_tensor("KT", [128, 4, L], BF16))
        Vres = e(nc.sbuf_tensor("Vres", [128, 32, 512], BF16))
        WqSB = e(nc.sbuf_tensor("WqSB", [128, 4, 8, 128], BF16))
        WkSB = e(nc.sbuf_tensor("WkSB", [128, 4, 8, 128], BF16))
        WvSB = e(nc.sbuf_tensor("WvSB", [128, 8, 512], BF16))
        bqSB = e(nc.sbuf_tensor("bqSB", [128, 4], F32))
        bkSB = e(nc.sbuf_tensor("bkSB", [128, 4], F32))
        bvRow = e(nc.sbuf_tensor("bvRow", [1, 512], BF16))
        ident = e(nc.sbuf_tensor("ident", [128, 128], BF16))
        ones = e(nc.sbuf_tensor("ones", [1, 128], BF16))
        s16 = e(nc.sbuf_tensor("s16", [128, L], F32))
        s16m = e(nc.sbuf_tensor("s16m", [128, 256], F32))
        e_bf = e(nc.sbuf_tensor("e_bf", [128, L], BF16))
        p_bf = e(nc.sbuf_tensor("p_bf", [128, L], BF16))
        pT = e(nc.sbuf_tensor("pT", [128, 32, 128], BF16))
        cand = e(nc.sbuf_tensor("cand", [128, 256], F32))
        r8b = e(nc.sbuf_tensor("r8b", [128, 8], F32))
        selst = e(nc.sbuf_tensor("selst", [128, 8], F32))
        t16 = e(nc.sbuf_tensor("t16", [128, 1], F32))
        ctx_o = e(nc.sbuf_tensor("ctx_o", [128, 2, 64], BF16))
        psMM = [e(nc.psum_tensor(f"psMM{i}", [128, 512], F32)) for i in range(2)]
        psTR = e(nc.psum_tensor("psTR", [128, 1024], BF16))
        psV = e(nc.psum_tensor("psV", [128, 64], F32))

        sd = Sem(stack, nc, "sd")
        gw = Sem(stack, nc, "gw")
        st = Sem(stack, nc, "st")
        sv = Sem(stack, nc, "sv")
        sa = Sem(stack, nc, "sa")
        sdc = [Sem(stack, nc, "sdc0"), Sem(stack, nc, "sdc1")]

        def hq(t, h, cols):
            return t[(h % 2) * 64:(h % 2) * 64 + 64, h // 2, cols]

        with nc.Block() as block:
            @block.sync
            def _(sync):
                for dst, src_ in [(bqSB[:], BQ), (bkSB[:], BK), (ident[:], ident_c),
                                  (ones[:], ones_c)]:
                    sd.inc(sync.dma_start(out=dst, in_=src_), 16)
                sync.wait_ge(sd.h, sd.v)

            @block.gpsimd
            def _(g):
                gw.inc(g.dma_start(out=WqSB[:], in_=WQ.rearrange("dc p k c -> p dc k c")), 16)
                gw.inc(g.dma_start(out=WkSB[:], in_=WK.rearrange("dc p k c -> p dc k c")), 16)
                gw.inc(g.dma_start(out=WvSB[:], in_=WV.rearrange("k p c -> p k c")), 16)
                gw.inc(g.dma_start(out=bvRow[:], in_=BVr), 16)
                g.wait_ge(gw.h, gw.v)

            pre_sd, pre_gw = sd.v, gw.v
            for mk in (block.tensor, block.vector, block.scalar):
                def _f(eng):
                    eng.wait_ge(sd.h, pre_sd)
                    eng.wait_ge(gw.h, pre_gw)
                mk(_f)

            sdP, stP, svP, saP = sd.v, st.v, sv.v, sa.v

            @block.sync
            def _(sync):
                for t in range(NT):
                    if t >= 2:
                        sv.wait(sync, svP + 5 * (t - 2) + 4)
                    sync.dma_start(out=x_t[:, t % 2, :],
                                   in_=x1[t * 128:(t + 1) * 128, :]).then_inc(sd.h, 16)
            sd.v = sdP + NT * 16

            @block.vector
            def _(vector):
                for t in range(NT):
                    vector.wait_ge(sd.h, sdP + 16 * min(t + 2, NT))
                    sv.inc(vector.tensor_reduce(out=st8[:, 0:1], in_=x_t[:, t % 2, :],
                                                axis=AX.X, op=ALU.add))
                    sv.wait(vector)
                    sv.inc(vector.tensor_scalar(out=st8[:, 1:2], in0=st8[:, 0:1],
                                                scalar1=-1.0 / D, scalar2=None,
                                                op0=ALU.mult))
                    sa.wait(vector, saP + 12 * t + 1)
                    sv.inc(vector.tensor_scalar(out=st8[:, 3:4], in0=st8[:, 2:3],
                                                scalar1=1.0 / D, scalar2=1e-5,
                                                op0=ALU.mult, op1=ALU.add))
                    sa.wait(vector, saP + 12 * t + 3)
                    if t >= 2:
                        st.wait(vector, stP + 10 * (t - 2) + 1)
                    sv.inc(vector.tensor_scalar(out=xn_bf[:, t % 2, :],
                                                in0=x_t[:, t % 2, :],
                                                scalar1=st8[:, 1:2], scalar2=st8[:, 5:6],
                                                op0=ALU.add, op1=ALU.mult))
                    st.wait(vector, stP + 10 * t + 1)
                    sv.inc(vector.tensor_copy(
                        out=slab[:, t % 2, :, :],
                        in_=psTR[:].rearrange("p (k c) -> p k c", k=8)))

            @block.scalar
            def _(scalar):
                for t in range(NT):
                    sv.wait(scalar, svP + 5 * t + 2)
                    sa.inc(scalar.activation(out=sq[:], in_=x_t[:, t % 2, :],
                                             func=AF.Square, bias=st8[:, 1:2],
                                             accum_out=st8[:, 2:3]))
                    sv.wait(scalar, svP + 5 * t + 3)
                    sa.inc(scalar.activation(out=st8[:, 4:5], in_=st8[:, 3:4],
                                             func=AF.Ln))
                    sa.wait(scalar)
                    sa.inc(scalar.activation(out=st8[:, 5:6], in_=st8[:, 4:5],
                                             func=AF.Exp, scale=-0.5))
                    cols = slice(t * 128, (t + 1) * 128)
                    for gi in range(8):
                        prj, dc = gi // 4, gi % 4
                        st.wait(scalar, stP + 10 * t + 2 + gi)
                        dest = QT if prj == 0 else KT
                        bias = bqSB if prj == 0 else bkSB
                        sa.inc(scalar.activation(out=dest[:, dc, cols],
                                                 in_=psMM[gi % 2][:, :128],
                                                 func=AF.Identity,
                                                 bias=bias[:, dc:dc + 1]))
                    st.wait(scalar, stP + 10 * t + 10)
                    sa.inc(scalar.copy(out=Vres[:, t, :], in_=psMM[0][:, :512]))

            @block.tensor
            def _(tensor):
                for t in range(NT):
                    sv.wait(tensor, svP + 5 * t + 4)
                    if t >= 1:
                        sv.wait(tensor, svP + 5 * (t - 1) + 5)
                    for dc in range(8):
                        ins = tensor.transpose(psTR[:, dc * 128:(dc + 1) * 128],
                                               xn_bf[:, t % 2, dc * 128:(dc + 1) * 128],
                                               ident[:])
                    st.inc(ins)
                    sv.wait(tensor, svP + 5 * t + 5)
                    sa.wait(tensor, saP + 12 * t)
                    for gi in range(8):
                        prj, dc = gi // 4, gi % 4
                        if gi >= 2:
                            sa.wait(tensor, saP + 12 * t + 3 + gi - 1)
                        wsb = WqSB if prj == 0 else WkSB
                        for k in range(8):
                            ins = tensor.matmul(psMM[gi % 2][:, :128],
                                                wsb[:, dc, k, :],
                                                slab[:, t % 2, k, :],
                                                start=(k == 0), stop=(k == 7))
                        st.inc(ins)
                    sa.wait(tensor, saP + 12 * t + 10)
                    tensor.matmul(psMM[0][:, :512], ones[:], bvRow[:],
                                  start=True, stop=False)
                    for k in range(8):
                        ins = tensor.matmul(psMM[0][:, :512], slab[:, t % 2, k, :],
                                            WvSB[:, k, :], start=False, stop=(k == 7))
                    st.inc(ins)

            sdC, stC, svC, saC = sd.v, st.v, sv.v, sa.v

            def Sb(i): return stC + 13 * i
            def Ab(i): return saC + 10 * i
            def Vb(i): return svC + 67 * i

            @block.tensor
            def _(tensor):
                for h in range(8):
                    for qt in range(32):
                        i = h * 32 + qt
                        qcols = slice(qt * 128, (qt + 1) * 128)
                        for kc in range(8):
                            if kc < 2:
                                sa.wait(tensor, Ab(i))
                            else:
                                sa.wait(tensor, Ab(i) + kc - 1)
                            ins = tensor.matmul(psMM[kc % 2][:, :512],
                                                hq(QT, h, qcols),
                                                hq(KT, h, slice(kc * 512, (kc + 1) * 512)),
                                                start=True, stop=True)
                            st.inc(ins)
                        for g in range(4):
                            sv.wait(tensor, Vb(i) + 62)
                            if g >= 1:
                                sv.wait(tensor, Vb(i) + 63 + g)
                            for j in range(8):
                                kk = g * 8 + j
                                ins = tensor.transpose(psTR[:, j * 128:(j + 1) * 128],
                                                       p_bf[:, kk * 128:(kk + 1) * 128],
                                                       ident[:])
                            st.inc(ins)
                        sv.wait(tensor, Vb(i) + 67)
                        for kc in range(32):
                            ins = tensor.matmul(psV[:], pT[:, kc, :],
                                                Vres[:, kc, h * 64:(h + 1) * 64],
                                                start=(kc == 0), stop=(kc == 31))
                        st.inc(ins)

            @block.scalar
            def _(scalar):
                for h in range(8):
                    for qt in range(32):
                        i = h * 32 + qt
                        for kc in range(8):
                            st.wait(scalar, Sb(i) + kc + 1)
                            if kc == 0 and i >= 1:
                                sv.wait(scalar, Vb(i - 1) + 62)
                            sa.inc(scalar.activation(
                                out=s16[:, kc * 512:(kc + 1) * 512],
                                in_=psMM[kc % 2][:, :512], func=AF.Copy,
                                scale=0.125))
                        sv.wait(scalar, Vb(i) + 50)
                        sa.inc(scalar.activation(out=e_bf[:], in_=s16[:], func=AF.Exp,
                                                 bias=selst[:, 1:2]))
                        st.wait(scalar, Sb(i) + 13)
                        sv.wait(scalar, Vb(i) + 63)
                        if i >= 2:
                            scalar.wait_ge(sdc[i % 2].h, 16 * (i // 2))
                        sa.inc(scalar.activation(out=ctx_o[:, i % 2, :], in_=psV[:],
                                                 func=AF.Copy,
                                                 scale=selst[:, 4:5]))

            @block.vector
            def _(vector):
                for h in range(8):
                    for qt in range(32):
                        i = h * 32 + qt
                        vector.wait_ge(sa.h, Ab(i) + 8)
                        for ch in range(16):
                            ccol = slice(ch * 512, (ch + 1) * 512)
                            sv.inc(vector.max(out=cand[:, ch * 16:ch * 16 + 8],
                                              in_=s16[:, ch * 256:(ch + 1) * 256]
                                              if False else s16[:, ch * 256:(ch + 1) * 256]))
                            sv.wait(vector)
                            sv.inc(vector.match_replace(out=s16m[:],
                                                        in_to_replace=cand[:, ch * 16:ch * 16 + 8],
                                                        in_values=s16[:, ch * 256:(ch + 1) * 256],
                                                        imm_value=NEG))
                            sv.wait(vector)
                            sv.inc(vector.max(out=cand[:, ch * 16 + 8:ch * 16 + 16],
                                              in_=s16m[:]))
                        sv.wait(vector)
                        sv.inc(vector.tensor_reduce(out=selst[:, 0:1], in_=cand[:],
                                                    axis=AX.X, op=ALU.max))
                        sv.wait(vector)
                        sv.inc(vector.tensor_scalar(out=selst[:, 1:2], in0=selst[:, 0:1],
                                                    scalar1=-1.0, scalar2=None,
                                                    op0=ALU.mult))
                        for r in range(5):
                            sv.inc(vector.max(out=r8b[:], in_=cand[:]))
                            sv.wait(vector)
                            sv.inc(vector.match_replace(out=cand[:], in_to_replace=r8b[:],
                                                        in_values=cand[:],
                                                        imm_value=NEG))
                            sv.wait(vector)
                        sv.inc(vector.tensor_reduce(out=t16[:], in_=cand[:],
                                                    axis=AX.X, op=ALU.max))
                        sv.wait(vector)
                        vector.wait_ge(sa.h, Ab(i) + 9)
                        sv.inc(vector.scalar_tensor_tensor(out=p_bf[:], in0=s16[:],
                                                           scalar=t16[:], in1=e_bf[:],
                                                           op0=ALU.is_ge, op1=ALU.mult,
                                                           accum_out=selst[:, 3:4]))
                        sv.wait(vector)
                        sv.inc(vector.reciprocal(out=selst[:, 4:5],
                                                 in_=selst[:, 3:4]))
                        for g in range(4):
                            st.wait(vector, Sb(i) + 9 + g)
                            sv.inc(vector.tensor_copy(
                                out=pT[:, g * 8:(g + 1) * 8, :],
                                in_=psTR[:].rearrange("p (k c) -> p k c", k=8)))

            @block.sync
            def _(sync):
                for h in range(8):
                    for qt in range(32):
                        i = h * 32 + qt
                        sa.wait(sync, Ab(i) + 10)
                        sdc[i % 2].inc(
                            sync.dma_start(out=ctxo[qt * 128:(qt + 1) * 128,
                                                    h * 64:(h + 1) * 64],
                                           in_=ctx_o[:, i % 2, :]), 16)

            @block.sync
            def _(sync):
                sd.wait(sync)
                sync.wait_ge(sdc[0].h, sdc[0].v)
                sync.wait_ge(sdc[1].h, sdc[1].v)
                sync.wait_ge(gw.h, gw.v)

    return nc


def host_prep_C(inputs, x1_full, b, hg):
    cols = slice(hg * 512, (hg + 1) * 512)
    WqT = np.ascontiguousarray(inputs["L1_attn_Wq"].T[:, cols])
    WkT = np.ascontiguousarray(inputs["L1_attn_Wk"].T[:, cols])
    WvT = np.ascontiguousarray(inputs["L1_attn_Wv"].T[:, cols])
    return {
        "x1": np.ascontiguousarray(x1_full[b], dtype=np.float32),
        "WQ": prep_lhsT(WqT),
        "WK": prep_lhsT(WkT),
        "WV": prep_rhs(WvT),
        "BQ": prep_pvec(np.ascontiguousarray(inputs["L1_attn_bq"][cols])),
        "BK": prep_pvec(np.ascontiguousarray(inputs["L1_attn_bk"][cols])),
        "BVr": np.ascontiguousarray(inputs["L1_attn_bv"][cols].reshape(1, 512)),
    }


def kernel(**inputs):
    x = np.asarray(inputs["L0_attn_x"], dtype=np.float32)

    ncA = build_A()
    imA = [host_prep_A(inputs, c // 2, c % 2) for c in range(NCORE)]
    resA = run_bass_kernel_spmd(ncA, imA, core_ids=list(range(NCORE)))
    yf = np.empty((B, L, D), dtype=BF)
    for c in range(NCORE):
        b, hg = c // 2, c % 2
        yf[b][:, hg * 512:(hg + 1) * 512] = np.asarray(resA.results[c]["y"])

    ncB = build_B()
    imB = []
    for c in range(NCORE):
        b, j = c // 2, c % 2
        m = host_prep_B(inputs, b, j, yfull=yf)
        imB.append(m)
    resB = run_bass_kernel_spmd(ncB, imB, core_ids=list(range(NCORE)))
    x1f = np.empty((B, L, D), dtype=np.float32)
    for c in range(NCORE):
        b, j = c // 2, c % 2
        x1f[b][j * HALF:(j + 1) * HALF] = resB.results[c]["x1"]

    ncC = build_C()
    imC = [host_prep_C(inputs, x1f, c // 2, c % 2) for c in range(NCORE)]
    resC = run_bass_kernel_spmd(ncC, imC, core_ids=list(range(NCORE)))
    ctxf = np.empty((B, L, D), dtype=BF)
    for c in range(NCORE):
        b, hg = c // 2, c % 2
        ctxf[b][:, hg * 512:(hg + 1) * 512] = np.asarray(resC.results[c]["ctxo"])

    ncD = build_D()
    imD = [host_prep_D(inputs, x1f, ctxf, c // 2, c % 2) for c in range(NCORE)]
    resD = run_bass_kernel_spmd(ncD, imD, core_ids=list(range(NCORE)))
    out = np.empty((B, L, D), dtype=np.float32)
    for c in range(NCORE):
        b, j = c // 2, c % 2
        out[b][j * HALF:(j + 1) * HALF] = resD.results[c]["out"]
    return out
